# version 10 (final)
# speedup vs baseline: 1.2934x; 1.0025x over previous
import math
from contextlib import ExitStack
import numpy as np
import ml_dtypes
import concourse.bass as bass
import concourse.mybir as mybir
from concourse.bass_utils import run_bass_kernel_spmd

F32 = mybir.dt.float32
BF16 = mybir.dt.bfloat16
AF = mybir.ActivationFunctionType
ALU = mybir.AluOpType

B = 4
N = 2048
D = 2048
HEADS = 16
HD = 128
CH = 128
T = 1024
NCH = 8
KC = 16
TH = 2
INTER = 5632
IC = 44
NG = 4
GH = 4
EPS_LN = 1e-5
EPS_GN = 1e-6
BASE = 10000.0

LOG_LAM = [-(2.0 ** (-8.0 * (h + 1) / 16.0)) for h in range(HEADS)]
LAM_C = [float(np.float32(math.exp(LOG_LAM[h] * CH))) for h in range(HEADS)]
LAM_POW = [[float(np.float32(np.float64(math.exp(LOG_LAM[h] * CH))) ** c) for c in range(NCH)]
           for h in range(HEADS)]


def _f32(a):
    return np.ascontiguousarray(np.asarray(a), dtype=np.float32)


def host_tables(half):
    t = {}
    j = np.arange(64, dtype=np.float64)
    theta = BASE ** (-2.0 * j / 128.0)
    pos = np.arange(T, dtype=np.float64) + half * T
    ang = pos[None, :] * theta[:, None]
    c, s = np.cos(ang), np.sin(ang)
    t["cos2"] = _f32(np.concatenate([c, c], axis=0))
    t["sin2"] = _f32(np.concatenate([-s, s], axis=0))
    i = np.arange(CH, dtype=np.float64)
    dm = np.zeros((CH, HEADS * CH))
    dq = np.zeros((CH, HEADS * CH))
    dkv = np.zeros((CH, HEADS))
    for h in range(HEADS):
        ll = LOG_LAM[h]
        diff = i[None, :] - i[:, None]
        dm[:, h * CH:(h + 1) * CH] = np.where(diff >= 0, np.exp(ll * diff), 0.0)
        dq[:, h * CH:(h + 1) * CH] = np.exp(ll * (i[None, :] + 1.0))
        dkv[:, h] = np.exp(ll * (CH - 1.0 - i))
    t["dmaskT"] = _f32(dm)
    t["dqB"] = _f32(dq)
    t["dkv"] = _f32(dkv)
    t["ident"] = _f32(np.eye(128))
    t["sel"] = _f32(np.full((128, 1), 1.0 if half == 1 else 0.0))
    t["epsln"] = _f32(np.full((128, 1), EPS_LN))
    t["epsgn"] = _f32(np.full((128, 1), EPS_GN))
    oc = np.zeros((128, HEADS * 16))
    sr = np.zeros((16, HEADS * 128))
    for h in range(HEADS):
        oc[:, h * 16 + h] = 1.0
        sr[h, h * 128:(h + 1) * 128] = 1.0
    t["onescol"] = _f32(oc)
    t["selrow"] = _f32(sr)
    return t


def host_weights(inp):
    w = {}
    Wq = np.asarray(inp["L0_attn_Wq"])
    Wk = np.asarray(inp["L0_attn_Wk"])
    Wv = np.asarray(inp["L0_attn_Wv"])
    Wo = np.asarray(inp["L0_attn_Wo"])
    w["wq"] = _f32(Wq.reshape(16, 128, 16, 128).transpose(2, 1, 0, 3).reshape(16, 128, 2048))
    w["wk"] = _f32(Wk.reshape(16, 128, 16, 128).transpose(2, 1, 0, 3).reshape(16, 128, 2048))
    w["wv"] = _f32(Wv.reshape(16, 128, 4, 512).transpose(2, 1, 0, 3).reshape(4, 128, 16 * 512))
    w["wo"] = _f32(Wo.reshape(16, 128, 16, 128).transpose(2, 1, 0, 3).reshape(16, 128, 2048))
    w["wg1"] = _f32(np.asarray(inp["L0_attn_Wg1"]).reshape(16, 128, 128).transpose(1, 0, 2).reshape(128, 2048))
    w["wg2"] = _f32(np.asarray(inp["L0_attn_Wg2"]))
    gw = np.asarray(inp["L0_ffn_gate_w"])
    uw = np.asarray(inp["L0_ffn_up_w"])
    dw = np.asarray(inp["L0_ffn_down_w"])
    w["gw"] = _f32(gw.reshape(44, 128, 16, 128).transpose(0, 3, 2, 1).reshape(44, 128, 2048))
    w["uw"] = _f32(uw.reshape(44, 128, 16, 128).transpose(0, 3, 2, 1).reshape(44, 128, 2048))
    w["dwn"] = _f32(dw.reshape(16, 128, 44, 128).transpose(0, 3, 2, 1).reshape(16, 128, 44 * 128))
    w["gsv"] = _f32(np.asarray(inp["L0_ffn_gate_s"]).reshape(44, 128).T)
    w["usv"] = _f32(np.asarray(inp["L0_ffn_up_s"]).reshape(44, 128).T)
    w["dsv"] = _f32(np.asarray(inp["L0_ffn_down_s"]).reshape(16, 128).T)
    w["normw"] = _f32(np.asarray(inp["L0_attn_norm_weight"]).reshape(16, 128).T)
    return w


class Sem:
    def __init__(self, nc, es, name):
        self.h = es.enter_context(nc.semaphore(name))
        self.v = 0

    def inc(self, instr, n=1):
        instr.then_inc(self.h, n)
        self.v += n
        return self.v

    def dma(self, instr):
        return self.inc(instr, 16)


class DSem:

    def __init__(self, nc, es, name, n=2):
        self.s = [es.enter_context(nc.semaphore(f"{name}_{i}")) for i in range(n)]
        self.v = [0] * n
        self.n = n
        self.tgt = {}

    def dma(self, instr, idx):
        p = idx % self.n
        instr.then_inc(self.s[p], 16)
        self.v[p] += 16
        self.tgt[idx] = (p, self.v[p])

    def wait(self, eng, idx):
        p, val = self.tgt[idx]
        eng.wait_ge(self.s[p], val)

    def waitv(self, eng, idx, per=1):
        eng.wait_ge(self.s[idx % self.n], 16 * per * (idx // self.n + 1))

    def wait_all(self, eng):
        for p in range(self.n):
            if self.v[p]:
                eng.wait_ge(self.s[p], self.v[p])


def build_nc(debug=False, stage=8, no_cc=False, no_rot=False, no_scan=False,
             ngroups=NG, parts=15, late_exchange=True, no_prepass=True):
    nc = bass.Bass("TRN2", target_bir_lowering=False)
    es = ExitStack()

    din = lambda name, shape, dty=F32: nc.dram_tensor(name, shape, dty, kind="ExternalInput").ap()
    xT = din("xT", [D, T])
    wq_d = din("wq", [16, 128, 2048])
    wk_d = din("wk", [16, 128, 2048])
    wv_d = din("wv", [4, 128, 16 * 512])
    wo_d = din("wo", [16, 128, 2048])
    wg1_d = din("wg1", [128, 2048])
    wg2_d = din("wg2", [128, 2048])
    gw_d = din("gw", [44, 128, 2048])
    uw_d = din("uw", [44, 128, 2048])
    dwn_d = din("dwn", [16, 128, 44 * 128])
    gsv_d = din("gsv", [128, 44])
    usv_d = din("usv", [128, 44])
    dsv_d = din("dsv", [128, 16])
    normw_d = din("normw", [128, 16])
    cos2_d = din("cos2", [128, T])
    sin2_d = din("sin2", [128, T])
    dmaskT_d = din("dmaskT", [128, HEADS * 128])
    dqB_d = din("dqB", [128, HEADS * 128])
    dkv_d = din("dkv", [128, 16])
    ident_d = din("ident", [128, 128])
    sel_d = din("sel", [128, 1])
    epsln_d = din("epsln", [128, 1])
    epsgn_d = din("epsgn", [128, 1])
    onescol_d = din("onescol", [128, 256])
    selrow_d = din("selrow", [16, 2048])
    outT = nc.dram_tensor("outT", [D, T], F32, kind="ExternalOutput").ap()
    sattn_d = nc.dram_tensor("sattn", [16, 128, T], BF16,
                             kind="ExternalOutput" if debug else "Internal").ap()
    cc_in = [nc.dram_tensor(f"cc_in{g}", [128, GH * 128], F32) for g in range(NG)]
    cc_out = [nc.dram_tensor(f"cc_out{g}", [256, GH * 128], F32) for g in range(NG)]

    if debug:
        dbg_xn = nc.dram_tensor("dbg_xn", [128, KC * T], BF16, kind="ExternalOutput").ap()
        dbg_o = nc.dram_tensor("dbg_o", [128, HEADS * T], BF16, kind="ExternalOutput").ap()
        dbg_o2 = nc.dram_tensor("dbg_o2", [128, HEADS * T], BF16, kind="ExternalOutput").ap()
        dbg_sg = nc.dram_tensor("dbg_sg", [128, HEADS * T], BF16, kind="ExternalOutput").ap()
        dbg_xg = nc.dram_tensor("dbg_xg", [128, HEADS * T], BF16, kind="ExternalOutput").ap()
        dbg_q = nc.dram_tensor("dbg_q", [128, HEADS * T], BF16, kind="ExternalOutput").ap()
        dbg_k = nc.dram_tensor("dbg_k", [128, HEADS * T], BF16, kind="ExternalOutput").ap()
        dbg_sloc = nc.dram_tensor("dbg_sloc", [128, HEADS * 128], F32, kind="ExternalOutput").ap()
        dbg_sinit = nc.dram_tensor("dbg_sinit", [128, HEADS * 128], F32, kind="ExternalOutput").ap()

    sb = lambda name, shape, dty: es.enter_context(nc.sbuf_tensor(name, shape, dty))
    psm = lambda name, shape, dty: es.enter_context(nc.psum_tensor(name, shape, dty))

    t_selrow = sb("t_selrow", [16, 2048], BF16)
    t_onescol = sb("t_onescol", [128, 256], BF16)
    t_dkv = sb("t_dkv", [128, 16], F32)
    t_ident = sb("t_ident", [128, 128], BF16)
    t_sel = sb("t_sel", [128, 1], F32)
    t_epsln = sb("t_epsln", [128, 1], F32)
    t_epsgn = sb("t_epsgn", [128, 1], F32)
    t_normw = sb("t_normw", [128, 16], F32)
    t_gs = sb("t_gs", [128, 44], F32)
    t_us = sb("t_us", [128, 44], F32)
    t_ds = sb("t_ds", [128, 16], F32)
    ones_col = sb("ones_col", [128, 1], F32)
    ones_col_bf = sb("ones_col_bf", [128, 1], BF16)
    ones_row = sb("ones_row", [1, 128], F32)
    gn_rstd = sb("gn_rstd", [16, T], F32)
    xn = sb("xn", [128, KC * T], BF16)
    S = sb("S", [128, HEADS * 128], F32)
    Sbf = sb("Sbf", [128, HEADS * 128], BF16)
    wg1_sb = sb("wg1_sb", [128, 2048], BF16)
    wg2_sb = sb("wg2_sb", [128, 2048], BF16)

    big = ExitStack()
    t_cos2 = big.enter_context(nc.sbuf_tensor("t_cos2", [128, T], BF16))
    t_sin2 = big.enter_context(nc.sbuf_tensor("t_sin2", [128, T], BF16))
    t_dmask = big.enter_context(nc.sbuf_tensor("t_dmask", [128, HEADS * 128], BF16))
    t_dqB = big.enter_context(nc.sbuf_tensor("t_dqB", [128, HEADS * 128], BF16))
    wv_buf = big.enter_context(nc.sbuf_tensor("wv_buf", [128, 16 * 512], BF16))
    wkq_buf = big.enter_context(nc.sbuf_tensor("wkq_buf", [128, 2 * 2048], BF16))

    pb = [psm(f"pb{i}", [128, 512], F32) for i in range(7)]
    pbt = psm("pbt", [128, 1024], BF16)

    S_ = lambda name: Sem(nc, es, name)
    s_x = DSem(nc, es, "s_x")
    s_tab = S_("s_tab")
    s_w = DSem(nc, es, "s_w")
    s_wv = S_("s_wv")
    s_wg = S_("s_wg")
    s_wo = DSem(nc, es, "s_wo")
    s_gu = DSem(nc, es, "s_gu")
    s_dn = DSem(nc, es, "s_dn")
    s_sw = DSem(nc, es, "s_sw")
    s_slab = S_("s_slab")
    s_vcons = S_("s_vcons")
    s_wod = S_("s_wod")
    s_gud = S_("s_gud")
    s_dnd = S_("s_dnd")
    s_pe = S_("s_pe")
    s_act = S_("s_act")
    s_dve = S_("s_dve")
    s_scv = S_("s_scv")
    s_stt = S_("s_stt")
    s_sbf = S_("s_sbf")
    s_gp = S_("s_gp")
    s_gpw = [DSem(nc, es, f"s_gpw{g}", n=1) for g in range(NG)]
    s_sttx = S_("s_sttx")
    s_dbg = S_("s_dbg")
    s_sas = DSem(nc, es, "s_sas")
    s_x8 = DSem(nc, es, "s_x8")
    s_out = DSem(nc, es, "s_out")
    s_ccg = [S_(f"s_ccg{g}") for g in range(NG)]
    s_ffa = S_("s_ffa")
    s_ffd = S_("s_ffd")
    s_ffd2 = S_("s_ffd2")

    groups_rg = [[0, 1], [2, 3], [4, 5], [6, 7]]

    slab_list = []
    slab_cum = []
    for g_ in range(ngroups):
        if parts & 1:
            slab_list += [("k", 4 * g_ + hh_) for hh_ in range(GH)]
        if parts & 4:
            slab_list += [("q", 4 * g_ + hh_) for hh_ in range(GH)]
        slab_cum.append(len(slab_list))
    slab_pos = {sk: j for j, sk in enumerate(slab_list)}
    slab_next = [min(2, len(slab_list))]

    with nc.Block() as blk:
        @blk.gpsimd
        def _(g):
            for dst, src in [(t_cos2, cos2_d), (t_sin2, sin2_d), (t_dmask, dmaskT_d),
                             (t_dqB, dqB_d), (t_ident, ident_d)]:
                s_tab.dma(g.dma_start(out=dst[:], in_=src))
            for dst, src in [(t_dkv, dkv_d), (t_sel, sel_d), (t_epsln, epsln_d),
                             (t_epsgn, epsgn_d), (t_onescol, onescol_d), (t_selrow, selrow_d),
                             (t_normw, normw_d), (t_gs, gsv_d), (t_us, usv_d), (t_ds, dsv_d)]:
                s_tab.dma(g.dma_start(out=dst[:], in_=src))
            for j in range(min(2, len(slab_list))):
                kind_, hd_ = slab_list[j]
                src_ = wk_d[hd_] if kind_ == "k" else wq_d[hd_]
                s_w.dma(g.dma_start(out=wkq_buf[:, (j % 2) * 2048:(j % 2) * 2048 + 2048], in_=src_), j)
            s_wv.dma(g.dma_start(out=wv_buf[:], in_=wv_d[0]))
            s_wg.dma(g.dma_start(out=wg1_sb[:], in_=wg1_d))
            s_wg.dma(g.dma_start(out=wg2_sb[:], in_=wg2_d))

        @blk.vector
        def _(v):
            v.memset(ones_col[:], 1.0)
            v.memset(ones_col_bf[:], 1.0)
            v.memset(ones_row[:], 1.0)
            s_dve.inc(v.memset(S[:], 0.0))
        w_init = s_dve.v

        ln = ExitStack()
        xb = ln.enter_context(nc.sbuf_tensor("xb", [128, KC * T], BF16))
        s_xt = [Sem(nc, es, f"s_xt{i}") for i in range(KC)]
        rb_bf = ln.enter_context(nc.sbuf_tensor("rb_bf", [128, T], BF16))
        mb_bf = ln.enter_context(nc.sbuf_tensor("mb_bf", [128, T], BF16))
        sq = ln.enter_context(nc.sbuf_tensor("sq", [128, 4 * T], BF16))
        tmp512 = ln.enter_context(nc.sbuf_tensor("tmp512", [128, 512], F32))
        tmpb = ln.enter_context(nc.sbuf_tensor("tmpb", [128, 512], BF16))
        r_mean = ln.enter_context(nc.sbuf_tensor("r_mean", [1, T], F32))
        r_msq = ln.enter_context(nc.sbuf_tensor("r_msq", [1, T], F32))
        r_var = ln.enter_context(nc.sbuf_tensor("r_var", [1, T], F32))
        r_rstd = ln.enter_context(nc.sbuf_tensor("r_rstd", [1, T], F32))
        r_negmr = ln.enter_context(nc.sbuf_tensor("r_negmr", [1, T], F32))

        @blk.gpsimd
        def _(s):
            for kc in range(KC):
                s_xt[kc].dma(s.dma_start(out=xb[:, kc * T:kc * T + T],
                                         in_=xT[kc * 128:(kc + 1) * 128, :]))

        w_tabs = s_tab.v

        @blk.scalar
        def _(sc):
            sc.wait_ge(s_tab.h, w_tabs)
            for kc in range(KC):
                sc.wait_ge(s_xt[kc].h, 16)
                if kc >= 4:
                    sc.wait_ge(s_pe.h, kc - 3)
                s_act.inc(sc.activation(out=sq[:, (kc % 4) * T:(kc % 4) * T + T],
                                        in_=xb[:, kc * T:kc * T + T], func=AF.Square))

        @blk.tensor
        def _(t):
            t.wait_ge(s_dve.h, w_init)
            t.wait_ge(s_tab.h, w_tabs)
            for kc in range(KC):
                t.wait_ge(s_xt[kc].h, 16)
                for th in range(TH):
                    t.matmul(pb[0 + th][0:1, :], ones_col_bf[:],
                             xb[:, kc * T + th * 512:kc * T + th * 512 + 512],
                             start=(kc == 0), stop=(kc == KC - 1))
                t.wait_ge(s_act.h, kc + 1)
                for th in range(TH):
                    ins = t.matmul(pb[2 + th][0:1, :], ones_col_bf[:],
                                   sq[:, (kc % 4) * T + th * 512:(kc % 4) * T + th * 512 + 512],
                                   start=(kc == 0), stop=(kc == KC - 1))
                s_pe.inc(ins)
        w_stats = s_pe.v

        @blk.vector
        def _(v):
            v.wait_ge(s_tab.h, w_tabs)
            v.wait_ge(s_pe.h, w_stats)
            v.tensor_scalar(r_mean[0:1, 0:512], pb[0][0:1, :], 1.0 / D, None, ALU.mult)
            v.tensor_scalar(r_mean[0:1, 512:1024], pb[1][0:1, :], 1.0 / D, None, ALU.mult)
            v.tensor_scalar(r_msq[0:1, 0:512], pb[2][0:1, :], 1.0 / D, None, ALU.mult)
            v.tensor_scalar(r_msq[0:1, 512:1024], pb[3][0:1, :], 1.0 / D, None, ALU.mult)
            v.tensor_tensor(r_var[0:1, :], r_mean[0:1, :], r_mean[0:1, :], ALU.mult)
            s_dve.inc(v.scalar_tensor_tensor(r_var[0:1, :], r_var[0:1, :], -1.0, r_msq[0:1, :],
                                             ALU.mult, ALU.add))
        w_var = s_dve.v

        @blk.scalar
        def _(sc):
            sc.wait_ge(s_dve.h, w_var)
            sc.activation(out=r_rstd[0:1, :], in_=r_var[0:1, :], func=AF.Ln, bias=t_epsln[0:1, :])
            s_act.inc(sc.activation(out=r_rstd[0:1, :], in_=r_rstd[0:1, :], func=AF.Exp, scale=-0.5))
        w_rstd = s_act.v

        @blk.vector
        def _(v):
            v.wait_ge(s_act.h, w_rstd)
            v.tensor_tensor(r_negmr[0:1, :], r_mean[0:1, :], r_rstd[0:1, :], ALU.mult)
            s_dve.inc(v.tensor_scalar(r_negmr[0:1, :], r_negmr[0:1, :], -1.0, None, ALU.mult))
        w_negmr = s_dve.v

        @blk.tensor
        def _(t):
            t.wait_ge(s_dve.h, w_negmr)
            for th in range(TH):
                t.matmul(pb[4 + th][:, :], ones_row[:], r_rstd[0:1, th * 512:th * 512 + 512],
                         start=True, stop=True)
            for th in range(TH):
                ins = t.matmul(pb[0 + th][:, :], ones_row[:], r_negmr[0:1, th * 512:th * 512 + 512],
                               start=True, stop=True)
            s_pe.inc(ins)
        w_bcast = s_pe.v

        @blk.vector
        def _(v):
            v.wait_ge(s_pe.h, w_bcast)
            for th in range(TH):
                v.tensor_copy(rb_bf[:, th * 512:th * 512 + 512], pb[4 + th][:, :])
            for th in range(TH):
                ins = v.tensor_copy(mb_bf[:, th * 512:th * 512 + 512], pb[0 + th][:, :])
            s_dve.inc(ins)
        w_bevac = s_dve.v

        xn_base = s_dve.v

        @blk.vector
        def _(v):
            for kc in range(KC):
                for th in range(TH):
                    src = slice(kc * T + th * 512, kc * T + th * 512 + 512)
                    dst = slice(kc * T + th * 512, kc * T + th * 512 + 512)
                    v.tensor_tensor(tmpb[:, :], xb[:, src], rb_bf[:, th * 512:th * 512 + 512], ALU.mult)
                    ins = v.tensor_tensor(xn[:, dst], tmpb[:, :], mb_bf[:, th * 512:th * 512 + 512], ALU.add)
                s_dve.inc(ins)
        w_xn = s_dve.v

        if debug:
            @blk.sync
            def _(s):
                s.wait_ge(s_dve.h, w_xn)
                s_dbg.dma(s.dma_start(out=dbg_xn, in_=xn[:]))

        ln.close()

        if stage >= 2:
            mid = ExitStack()
            ofull = mid.enter_context(nc.sbuf_tensor("ofull", [128, HEADS * T], BF16))
            qdq_st = ExitStack()
            qdq = qdq_st.enter_context(nc.sbuf_tensor("qdq", [128, HEADS * T], BF16))

            grp = ExitStack()
            k_g = grp.enter_context(nc.sbuf_tensor("k_g", [128, GH * T], BF16))
            q_g = grp.enter_context(nc.sbuf_tensor("q_g", [128, GH * T], BF16))
            kp_g = grp.enter_context(nc.sbuf_tensor("kp_g", [128, GH * T], BF16))
            v_g = grp.enter_context(nc.sbuf_tensor("v_g", [128, NCH * 512], BF16))
            ksw = grp.enter_context(nc.sbuf_tensor("ksw", [128, 2 * T], BF16))
            rot1 = grp.enter_context(nc.sbuf_tensor("rot1", [128, T], BF16))
            rot2 = grp.enter_context(nc.sbuf_tensor("rot2", [128, T], BF16))
            sc_sb = grp.enter_context(nc.sbuf_tensor("sc_sb", [128, 256], BF16))
            Sx = grp.enter_context(nc.sbuf_tensor("Sx", [128, GH * 128], F32))

            rotidx = [0]

            def proj_fm(t, dst_sem_slab, slab_i, bank_a, bank_b, free_guard, xn_gate=None):
                s_w.wait(t, slab_i)
                for sem_h, val in free_guard:
                    t.wait_ge(sem_h, val)
                for th in range(TH):
                    bank = bank_a if th == 0 else bank_b
                    for kc in range(KC):
                        if xn_gate is not None and th == 0:
                            t.wait_ge(s_dve.h, xn_gate + kc + 1)
                        ins = t.matmul(bank[:, :],
                                       wkq_buf[:, (slab_i % 2) * 2048 + kc * 128:(slab_i % 2) * 2048 + kc * 128 + 128],
                                       xn[:, kc * T + th * 512:kc * T + th * 512 + 512],
                                       start=(kc == 0), stop=(kc == KC - 1))
                s_slab.inc(ins)
                return s_slab.v

            def rotate(v, dst, hh):
                if no_rot:
                    rotidx[0] += 1
                    return v.memset(rot1[:, 0:4], 0.0)
                par = rotidx[0] % 2
                sl = slice(hh * T, hh * T + T)
                s_sw.wait(v, rotidx[0])
                v.tensor_tensor(rot1[:, :], dst[:, sl], t_cos2[:], ALU.mult)
                v.tensor_tensor(rot2[:, :], ksw[:, par * T:par * T + T], t_sin2[:], ALU.mult)
                ins = v.tensor_tensor(dst[:, sl], rot1[:, :], rot2[:, :], ALU.add)
                rotidx[0] += 1
                return ins

            def swap_dma(s, src, hh, guard_val):
                if no_rot:
                    rotidx_sw[0] += 1
                    return
                par = rotidx_sw[0] % 2
                sl0 = slice(hh * T, hh * T + T)
                s.wait_ge(s_act.h, guard_val)
                if rotidx_sw[0] >= 2:
                    s.wait_ge(s_dve.h, rot_done_vals[rotidx_sw[0] - 2])
                s_sw.dma(s.dma_start(out=ksw[0:64, par * T:par * T + T], in_=src[64:128, sl0]), rotidx_sw[0])
                s_sw.dma(s.dma_start(out=ksw[64:128, par * T:par * T + T], in_=src[0:64, sl0]), rotidx_sw[0])
                rotidx_sw[0] += 1

            rotidx_sw = [0]
            rot_done_vals = []

            wgb = [0] * NG
            wsel = [0] * NG
            wsbi = [0] * NG

            def exchange_finish(gx):
                @blk.gpsimd
                def _(gp, gx=gx):
                    gp.wait_ge(s_ccg[gx].h, 1)
                    s_gpw[gx].dma(gp.dma_start(out=S[:, gx * 512:gx * 512 + 512], in_=cc_out[gx].ap()[0:128, :]), 0)
                    wgb[gx] = s_gpw[gx].v[0]

                @blk.vector
                def _(v, gx=gx):
                    v.wait_ge(s_gpw[gx].s[0], wgb[gx])
                    ins = v.tensor_scalar(S[:, gx * 512:gx * 512 + 512], S[:, gx * 512:gx * 512 + 512],
                                          t_sel[:], None, ALU.mult)
                    s_dve.inc(ins)
                    wsel[gx] = s_dve.v

                @blk.scalar
                def _(sc, gx=gx):
                    sc.wait_ge(s_dve.h, wsel[gx])
                    ins = sc.activation(out=Sbf[:, gx * 512:gx * 512 + 512],
                                        in_=S[:, gx * 512:gx * 512 + 512], func=AF.Copy)
                    s_sbf.inc(ins)
                    wsbi[gx] = s_sbf.v

            scv_base = 0
            stt_base = 0
            prev_kp_done = [0]
            dbg_kg_done = [0]

            for g in range(ngroups):
                @blk.gpsimd
                def _(gp, g=g):
                    target = min(len(slab_list), slab_cum[g] + 2)
                    while slab_next[0] < target:
                        j = slab_next[0]
                        gp.wait_ge(s_slab.h, j - 1)
                        kind_, hd_ = slab_list[j]
                        src_ = wk_d[hd_] if kind_ == "k" else wq_d[hd_]
                        s_w.dma(gp.dma_start(out=wkq_buf[:, (j % 2) * 2048:(j % 2) * 2048 + 2048], in_=src_), j)
                        slab_next[0] += 1

                ek = [0] * GH
                eq = [0] * GH
                ev = [0] * NCH
                pk = [0] * GH
                pq = [0] * GH
                pv = [0] * NCH
                ekp = [0] * GH
                rotk_w = [0] * GH
                rotq_w = [0] * GH
                qdq_w = [0] * GH

                for hh in range(GH if parts & 1 else 0):
                    slab_i = slab_pos[("k", 4 * g + hh)]
                    banks = (pb[2], pb[3]) if hh % 2 == 0 else (pb[4], pb[5])

                    @blk.tensor
                    def _(t, hh=hh, slab_i=slab_i, banks=banks, g=g):
                        guards = []
                        gate = None
                        if g == 0 and hh < 2:
                            guards.append((s_dve.h, w_bevac))
                            guards.append((s_tab.h, s_tab.v))
                            gate = xn_base
                        if g > 0 and hh < 2:
                            guards.append((s_scv.h, scv_base))
                            guards.append((s_stt.h, stt_base))
                        if hh >= 2:
                            guards.append((s_act.h, ek[hh - 2]))
                        proj_fm(t, s_slab, slab_i, banks[0], banks[1], guards, xn_gate=gate)
                        pk[hh] = s_slab.v

                    @blk.scalar
                    def _(sc, hh=hh, banks=banks, g=g):
                        sc.wait_ge(s_slab.h, pk[hh])
                        if debug and g == 1 and hh == 0:
                            sc.wait_ge(s_dbg.h, dbg_kg_done[0])
                        for th in range(TH):
                            ins = sc.activation(out=k_g[:, hh * T + th * 512:hh * T + th * 512 + 512],
                                                in_=banks[th][:, :], func=AF.Silu)
                        s_act.inc(ins)
                        ek[hh] = s_act.v

                    @blk.sync
                    def _(s, hh=hh):
                        swap_dma(s, k_g, hh, ek[hh])

                    @blk.vector
                    def _(v, hh=hh):
                        ins = rotate(v, k_g, hh)
                        s_dve.inc(ins)
                        rotk_w[hh] = s_dve.v
                        rot_done_vals.append(s_dve.v)

                for hh in range(GH if parts & 4 else 0):
                    slab_i = slab_pos[("q", 4 * g + hh)]
                    banks = (pb[2], pb[3]) if hh % 2 == 0 else (pb[4], pb[5])

                    @blk.tensor
                    def _(t, hh=hh, slab_i=slab_i, banks=banks):
                        guards = [(s_act.h, eq[hh - 2] if hh >= 2 else ek[GH - 1])]
                        proj_fm(t, s_slab, slab_i, banks[0], banks[1], guards)
                        pq[hh] = s_slab.v

                    @blk.scalar
                    def _(sc, hh=hh, banks=banks):
                        sc.wait_ge(s_slab.h, pq[hh])
                        for th in range(TH):
                            ins = sc.activation(out=q_g[:, hh * T + th * 512:hh * T + th * 512 + 512],
                                                in_=banks[th][:, :], func=AF.Silu)
                        s_act.inc(ins)
                        eq[hh] = s_act.v

                    @blk.sync
                    def _(s, hh=hh):
                        swap_dma(s, q_g, hh, eq[hh])

                    @blk.vector
                    def _(v, hh=hh, g=g):
                        ins = rotate(v, q_g, hh)
                        s_dve.inc(ins)
                        rotq_w[hh] = s_dve.v
                        rot_done_vals.append(s_dve.v)
                        h = 4 * g + hh
                        rep = bass.AP(t_dqB, h * 128, [[HEADS * 128, 128], [0, NCH], [1, 128]])
                        ins = v.tensor_tensor(qdq[:, h * T:h * T + T], q_g[:, hh * T:hh * T + T], rep, ALU.mult)
                        s_dve.inc(ins)
                        qdq_w[hh] = s_dve.v

                for c in range(NCH if parts & 2 else 0):
                    bank = pb[6] if c % 2 == 0 else pb[0]

                    @blk.tensor
                    def _(t, c=c, bank=bank, g=g):
                        t.wait_ge(s_wv.h, 16 * (g + 1))
                        if g == 0 and c == 0:
                            t.wait_ge(s_dve.h, w_xn)
                        if c >= 2:
                            t.wait_ge(s_act.h, ev[c - 2])
                        if c < 2 and g > 0:
                            t.wait_ge(s_act.h, oev_last[0])
                        for kc in range(KC):
                            ins = t.matmul(bank[:, :],
                                           xn[:, kc * T + c * 128:kc * T + c * 128 + 128],
                                           wv_buf[:, kc * 512:kc * 512 + 512],
                                           start=(kc == 0), stop=(kc == KC - 1))
                        s_pe.inc(ins)
                        pv[c] = s_pe.v

                    @blk.scalar
                    def _(sc, c=c, bank=bank):
                        sc.wait_ge(s_pe.h, pv[c])
                        ins = sc.activation(out=v_g[:, c * 512:c * 512 + 512], in_=bank[:, :], func=AF.Copy)
                        s_act.inc(ins)
                        ev[c] = s_act.v

                if parts & 2:
                    @blk.gpsimd
                    def _(gp, g=g):
                        if g + 1 < ngroups:
                            gp.wait_ge(s_pe.h, pv[NCH - 1])
                            s_wv.dma(gp.dma_start(out=wv_buf[:], in_=wv_d[g + 1]))

                for hh in range(GH if parts & 8 else 0):
                    @blk.tensor
                    def _(t, hh=hh, g=g):
                        t.wait_ge(s_dve.h, rotk_w[hh])
                        if hh >= 1:
                            t.wait_ge(s_act.h, ekp[hh - 1])
                        elif g > 0:
                            t.wait_ge(s_act.h, prev_kp_done[0])
                        for c in range(NCH):
                            ins = t.transpose(pbt[:, c * 128:c * 128 + 128],
                                              k_g[:, hh * T + c * 128:hh * T + c * 128 + 128], t_ident[:])
                        s_pe.inc(ins)
                        ptr = s_pe.v
                        ekp[hh] = ptr

                    @blk.scalar
                    def _(sc, hh=hh, g=g):
                        sc.wait_ge(s_pe.h, ekp[hh])
                        h = 4 * g + hh
                        ins = sc.activation(out=kp_g[:, hh * T:hh * T + T], in_=pbt[:, 0:1024],
                                            func=AF.Copy, scale=t_dkv[:, h:h + 1])
                        s_act.inc(ins)
                        ekp[hh] = s_act.v
                        if hh == GH - 1:
                            prev_kp_done[0] = s_act.v

                if debug and g == 0:
                    @blk.sync
                    def _(s):
                        s.wait_ge(s_dve.h, max(max(rotq_w), max(rotk_w)))
                        s_dbg.dma(s.dma_start(out=dbg_k[0:128, 0:GH * T], in_=k_g[:]))
                    dbg_kg_done[0] = s_dbg.v

                if g == NG - 1 and not no_scan and not no_prepass:
                    ppx = [[0] * GH for _ in range(NCH)]

                    @blk.tensor
                    def _(t, g=g):
                        t.wait_ge(s_act.h, max(eq[GH - 1], max(ekp), ev[NCH - 1]))
                        for c in range(NCH):
                            for hh in range(GH):
                                if c >= 1:
                                    t.wait_ge(s_sttx.h, (c - 1) * GH + hh + 1)
                                ins = t.matmul(pb[2 + hh][:, 0:128],
                                               kp_g[:, hh * T + c * 128:hh * T + c * 128 + 128],
                                               v_g[:, c * 512 + hh * 128:c * 512 + hh * 128 + 128],
                                               start=True, stop=True)
                                s_pe.inc(ins)
                                ppx[c][hh] = s_pe.v

                    @blk.vector
                    def _(v, g=g):
                        for c in range(NCH):
                            for hh in range(GH):
                                h = 4 * g + hh
                                v.wait_ge(s_pe.h, ppx[c][hh])
                                sl = slice(hh * 128, hh * 128 + 128)
                                if c == 0:
                                    ins = v.tensor_copy(Sx[:, sl], pb[2 + hh][:, 0:128])
                                else:
                                    ins = v.scalar_tensor_tensor(Sx[:, sl], Sx[:, sl], LAM_C[h],
                                                                 pb[2 + hh][:, 0:128], ALU.mult, ALU.add)
                                s_sttx.inc(ins)

                    @blk.gpsimd
                    def _(gp, g=g):
                        gp.wait_ge(s_sttx.h, NCH * GH)
                        s_gp.dma(gp.dma_start(out=cc_in[g].ap(), in_=Sx[:]))
                        gp.wait_ge(s_gp.h, s_gp.v)
                        if not no_cc:
                            gp.wait_ge(s_ccg[g - 1].h, 1)
                            s_ccg[g].inc(gp.collective_compute(
                                "AllGather", ALU.bypass, replica_groups=groups_rg,
                                ins=[cc_in[g].ap()], outs=[cc_out[g].ap()]))

                if g == 0:
                    oev_last = [s_act.v]
                for hh in range(GH if not no_scan else 0):
                    h = 4 * g + hh
                    my_pe_sc = [0] * NCH
                    my_pe_intra = [0] * NCH
                    my_pe_ds = [0] * NCH
                    head_stt = [0] * NCH
                    head_sbf = [0] * NCH
                    my_scv_cnt_base = [s_scv.v]
                    my_stt_base = [s_stt.v]
                    my_sbf_base = [s_sbf.v]

                    @blk.tensor
                    def _(t, hh=hh, h=h, g=g):
                        t.wait_ge(s_act.h, max(ekp[hh], ev[NCH - 1]))
                        t.wait_ge(s_dve.h, qdq_w[hh])
                        if g == NG - 1 and hh == 0 and not no_prepass:
                            t.wait_ge(s_sttx.h, NCH * GH)
                        for c in range(NCH):
                            obank = pb[0] if c < 4 else pb[1]
                            osl = slice((c % 4) * 128, (c % 4) * 128 + 128)
                            if c >= 1:
                                t.wait_ge(s_scv.h, my_scv_cnt_base[0] + c)
                            elif hh > 0 or g > 0:
                                t.wait_ge(s_scv.h, my_scv_cnt_base[0])
                            ins = t.matmul(pb[2][:, 0:128],
                                           k_g[:, hh * T + c * 128:hh * T + c * 128 + 128],
                                           q_g[:, hh * T + c * 128:hh * T + c * 128 + 128],
                                           start=True, stop=True)
                            s_pe.inc(ins)
                            my_pe_sc[c] = s_pe.v
                            t.wait_ge(s_scv.h, my_scv_cnt_base[0] + c + 1)
                            if c < 4 and (hh > 0 or g > 0) and c == 0:
                                t.wait_ge(s_act.h, oev_last[0])
                            ins = t.matmul(obank[:, osl],
                                           v_g[:, c * 512 + hh * 128:c * 512 + hh * 128 + 128],
                                           sc_sb[:, (c % 2) * 128:(c % 2) * 128 + 128],
                                           start=True, stop=(c == 0))
                            s_pe.inc(ins)
                            my_pe_intra[c] = s_pe.v
                            if c >= 1:
                                t.wait_ge(s_sbf.h, my_sbf_base[0] + c)
                                ins = t.matmul(obank[:, osl],
                                               Sbf[:, h * 128:h * 128 + 128],
                                               qdq[:, h * T + c * 128:h * T + c * 128 + 128],
                                               start=False, stop=True)
                            if c >= 1:
                                t.wait_ge(s_stt.h, my_stt_base[0] + c)
                            ins = t.matmul(pb[3][:, 0:128],
                                           kp_g[:, hh * T + c * 128:hh * T + c * 128 + 128],
                                           v_g[:, c * 512 + hh * 128:c * 512 + hh * 128 + 128],
                                           start=True, stop=True)
                            s_pe.inc(ins)
                            my_pe_ds[c] = s_pe.v

                    @blk.vector
                    def _(v, hh=hh, h=h):
                        for c in range(NCH):
                            v.wait_ge(s_pe.h, my_pe_sc[c])
                            if c >= 2:
                                v.wait_ge(s_pe.h, my_pe_intra[c - 2])
                            ins = v.tensor_tensor(sc_sb[:, (c % 2) * 128:(c % 2) * 128 + 128],
                                                  pb[2][:, 0:128],
                                                  t_dmask[:, h * 128:h * 128 + 128], ALU.mult)
                            s_scv.inc(ins)
                            v.wait_ge(s_pe.h, my_pe_ds[c])
                            ins = v.scalar_tensor_tensor(S[:, h * 128:h * 128 + 128],
                                                         S[:, h * 128:h * 128 + 128], LAM_C[h],
                                                         pb[3][:, 0:128], ALU.mult, ALU.add)
                            s_stt.inc(ins)
                            head_stt[c] = s_stt.v

                    @blk.scalar
                    def _(sc, h=h):
                        for c in range(NCH - 1):
                            sc.wait_ge(s_stt.h, head_stt[c])
                            if c == 4:
                                sc.wait_ge(s_pe.h, my_pe_ds[3])
                                ins = sc.activation(out=ofull[:, h * T:h * T + 512],
                                                    in_=pb[0][:, :], func=AF.Copy)
                                s_act.inc(ins)
                            ins = sc.activation(out=Sbf[:, h * 128:h * 128 + 128],
                                                in_=S[:, h * 128:h * 128 + 128], func=AF.Copy)
                            s_sbf.inc(ins)
                            head_sbf[c] = s_sbf.v
                        sc.wait_ge(s_pe.h, my_pe_ds[7])
                        ins = sc.activation(out=ofull[:, h * T + 512:h * T + 1024],
                                            in_=pb[1][:, :], func=AF.Copy)
                        s_act.inc(ins)
                        oev_last[0] = s_act.v

                scv_base = s_scv.v
                stt_base = s_stt.v

                if g < NG - 1 or no_scan or no_prepass:
                    @blk.gpsimd
                    def _(gp, g=g):
                        gp.wait_ge(s_stt.h, stt_base)
                        s_gp.dma(gp.dma_start(out=cc_in[g].ap(), in_=S[:, g * 512:g * 512 + 512]))
                        gp.wait_ge(s_gp.h, s_gp.v)
                        if not no_cc:
                            if g > 0:
                                gp.wait_ge(s_ccg[g - 1].h, 1)
                            s_ccg[g].inc(gp.collective_compute(
                                "AllGather", ALU.bypass, replica_groups=groups_rg,
                                ins=[cc_in[g].ap()], outs=[cc_out[g].ap()]))

                if not debug and not late_exchange and not no_cc and g > 0 and stage >= 4:
                    exchange_finish(g - 1)

            w_dbg_ph2 = s_dbg.v
            if debug:
                @blk.sync
                def _(s):
                    s.wait_ge(s_stt.h, stt_base)
                    s_dbg.dma(s.dma_start(out=dbg_sloc, in_=S[:]))
                    s.wait_ge(s_dve.h, s_dve.v)
                    s_dbg.dma(s.dma_start(out=dbg_q, in_=qdq[:]))
                    s_dbg.dma(s.dma_start(out=dbg_o, in_=ofull[:]))
                w_dbg_ph2 = s_dbg.v

        if stage >= 4:
            if debug or late_exchange:
                if debug:
                    @blk.gpsimd
                    def _(gp):
                        gp.wait_ge(s_dbg.h, w_dbg_ph2)
                for gx in range(NG):
                    exchange_finish(gx)
            else:
                exchange_finish(NG - 1)

            if debug:
                @blk.sync
                def _(s):
                    s.wait_ge(s_dve.h, wsel[NG - 1])
                    s_dbg.dma(s.dma_start(out=dbg_sinit, in_=S[:]))

            pp2 = [0] * HEADS
            p2d = [0] * HEADS

            p2d_base = s_dve.v

            @blk.tensor
            def _(t):
                for h in range(HEADS):
                    banks = (pb[4], pb[5]) if h % 2 == 0 else (pb[2], pb[3])
                    t.wait_ge(s_sbf.h, wsbi[h // 4])
                    if h >= 2:
                        t.wait_ge(s_dve.h, p2d_base + h - 1)
                    for th in range(TH):
                        ins = t.matmul(banks[th][:, :], Sbf[:, h * 128:h * 128 + 128],
                                       qdq[:, h * T + th * 512:h * T + th * 512 + 512],
                                       start=True, stop=True)
                    s_pe.inc(ins)
                    pp2[h] = s_pe.v

            @blk.vector
            def _(v):
                if debug:
                    v.wait_ge(s_dbg.h, w_dbg_ph2)
                for h in range(HEADS):
                    banks = (pb[4], pb[5]) if h % 2 == 0 else (pb[2], pb[3])
                    v.wait_ge(s_pe.h, pp2[h])
                    for c in range(NCH):
                        bank = banks[c // 4]
                        ins = v.scalar_tensor_tensor(ofull[:, h * T + c * 128:h * T + c * 128 + 128],
                                                     bank[:, (c % 4) * 128:(c % 4) * 128 + 128],
                                                     LAM_POW[h][c],
                                                     ofull[:, h * T + c * 128:h * T + c * 128 + 128],
                                                     ALU.mult, ALU.add)
                    s_dve.inc(ins)
                    p2d[h] = s_dve.v

            w_dbg_o2 = s_dbg.v
            if debug:
                @blk.sync
                def _(s):
                    s.wait_ge(s_dve.h, p2d[HEADS - 1])
                    s_dbg.dma(s.dma_start(out=dbg_o2, in_=ofull[:]))
                w_dbg_o2 = s_dbg.v

            grp.close()
            qdq_st.close()

        if stage >= 5:
            ph56 = ExitStack()
            wo_buf = ph56.enter_context(nc.sbuf_tensor("wo_buf", [128, 2 * 2048], BF16))
            sa_stage = ph56.enter_context(nc.sbuf_tensor("sa_stage", [128, 2 * T], BF16))
            ph5 = ExitStack()
            sg_all = ph5.enter_context(nc.sbuf_tensor("sg_all", [128, HEADS * T], BF16))
            g1_sb = ph5.enter_context(nc.sbuf_tensor("g1_sb", [128, T], BF16))
            tscr = ph5.enter_context(nc.sbuf_tensor("tscr", [128, T], BF16))
            sqscr = ph5.enter_context(nc.sbuf_tensor("sqscr", [128, 2 * T], BF16))
            gn_rstd_bf = ph5.enter_context(nc.sbuf_tensor("gn_rstd_bf", [16, T], BF16))

            @blk.gpsimd
            def _(gp):
                for hc in range(16):
                    if hc >= 2:
                        gp.wait_ge(s_wod.h, hc - 1)
                    else:
                        gp.wait_ge(s_dve.h, p2d[HEADS - 1])
                    s_wo.dma(gp.dma_start(out=wo_buf[:, (hc % 2) * 2048:(hc % 2) * 2048 + 2048], in_=wo_d[hc]), hc)
                for ic0 in (0, 1):
                    s_gu.dma(gp.dma_start(out=wkq_buf[:, ic0 * 2048:ic0 * 2048 + 2048], in_=gw_d[ic0]), ic0)
                    s_gu.dma(gp.dma_start(out=wv_buf[:, ic0 * 2048:ic0 * 2048 + 2048], in_=uw_d[ic0]), ic0)

            pg2 = [0] * HEADS
            esg = [0] * HEADS

            @blk.tensor
            def _(t):
                t.wait_ge(s_wg.h, 32)
                t.wait_ge(s_dve.h, p2d[HEADS - 1])
                for th in range(TH):
                    for kc in range(KC):
                        ins = t.matmul(pb[th][:, :], wg1_sb[:, kc * 128:kc * 128 + 128],
                                       xn[:, kc * T + th * 512:kc * T + th * 512 + 512],
                                       start=(kc == 0), stop=(kc == KC - 1))
                s_pe.inc(ins)
            w_g1 = s_pe.v

            @blk.scalar
            def _(sc):
                sc.wait_ge(s_pe.h, w_g1)
                for th in range(TH):
                    ins = sc.activation(out=g1_sb[:, th * 512:th * 512 + 512], in_=pb[th][:, :], func=AF.Copy)
                s_act.inc(ins)
            w_eg1 = s_act.v

            @blk.tensor
            def _(t):
                t.wait_ge(s_act.h, w_eg1)
                for h in range(HEADS):
                    banks = (pb[2], pb[3]) if h % 2 == 0 else (pb[4], pb[5])
                    if h >= 2:
                        t.wait_ge(s_act.h, w_eg1 + h - 1)
                    for th in range(TH):
                        ins = t.matmul(banks[th][:, :], wg2_sb[:, h * 128:h * 128 + 128],
                                       g1_sb[:, th * 512:th * 512 + 512], start=True, stop=True)
                    s_pe.inc(ins)
                    pg2[h] = s_pe.v

            @blk.scalar
            def _(sc):
                for h in range(HEADS):
                    banks = (pb[2], pb[3]) if h % 2 == 0 else (pb[4], pb[5])
                    sc.wait_ge(s_pe.h, pg2[h])
                    for th in range(TH):
                        ins = sc.activation(out=sg_all[:, h * T + th * 512:h * T + th * 512 + 512],
                                            in_=banks[th][:, :], func=AF.Sigmoid)
                    s_act.inc(ins)
                    esg[h] = s_act.v

            if debug:
                @blk.sync
                def _(s):
                    s.wait_ge(s_act.h, esg[HEADS - 1])
                    s_dbg.dma(s.dma_start(out=dbg_sg, in_=sg_all[:]))

            wsq = [0] * HEADS
            pgns = [0] * HEADS

            pgn_base = s_pe.v

            @blk.vector
            def _(v):
                for h in range(HEADS):
                    v.wait_ge(s_act.h, esg[h])
                    if h >= 2:
                        v.wait_ge(s_pe.h, pgn_base + h - 1)
                    v.tensor_tensor(tscr[:], ofull[:, h * T:h * T + T], sg_all[:, h * T:h * T + T], ALU.mult)
                    ins = v.tensor_tensor(sqscr[:, (h % 2) * T:(h % 2) * T + T], tscr[:], tscr[:], ALU.mult)
                    s_dve.inc(ins)
                    wsq[h] = s_dve.v

            @blk.tensor
            def _(t):
                t.wait_ge(s_act.h, w_eg1)
                for h in range(HEADS):
                    t.wait_ge(s_dve.h, wsq[h])
                    for th in range(TH):
                        ins = t.matmul(pb[th][0:16, :], t_onescol[:, h * 16:h * 16 + 16],
                                       sqscr[:, (h % 2) * T + th * 512:(h % 2) * T + th * 512 + 512],
                                       start=(h == 0), stop=(h == HEADS - 1))
                    s_pe.inc(ins)
                    pgns[h] = s_pe.v
            w_gnsum = s_pe.v

            @blk.scalar
            def _(sc):
                sc.wait_ge(s_pe.h, w_gnsum)
                sc.activation(out=gn_rstd[0:16, 0:512], in_=pb[0][0:16, :], func=AF.Ln,
                              scale=1.0 / HD, bias=t_epsgn[0:16, :])
                sc.activation(out=gn_rstd[0:16, 512:1024], in_=pb[1][0:16, :], func=AF.Ln,
                              scale=1.0 / HD, bias=t_epsgn[0:16, :])
                sc.activation(out=gn_rstd[0:16, :], in_=gn_rstd[0:16, :], func=AF.Exp, scale=-0.5)
                ins = sc.activation(out=gn_rstd_bf[0:16, :], in_=gn_rstd[0:16, :], func=AF.Copy)
                s_act.inc(ins)
            w_gnr = s_act.v

            pbc = [0] * HEADS
            wxg = [0] * HEADS

            wxg_base = s_dve.v

            @blk.tensor
            def _(t):
                t.wait_ge(s_act.h, w_gnr)
                for h in range(HEADS):
                    banks = (pb[2], pb[3]) if h % 2 == 0 else (pb[4], pb[5])
                    if h >= 2:
                        t.wait_ge(s_dve.h, wxg_base + h - 1)
                    for th in range(TH):
                        ins = t.matmul(banks[th][:, :], t_selrow[:, h * 128:h * 128 + 128],
                                       gn_rstd_bf[0:16, th * 512:th * 512 + 512], start=True, stop=True)
                    s_pe.inc(ins)
                    pbc[h] = s_pe.v

            @blk.vector
            def _(v):
                if debug:
                    v.wait_ge(s_dbg.h, w_dbg_o2)
                for h in range(HEADS):
                    banks = (pb[2], pb[3]) if h % 2 == 0 else (pb[4], pb[5])
                    v.wait_ge(s_pe.h, pbc[h])
                    v.tensor_tensor(tscr[:], ofull[:, h * T:h * T + T], sg_all[:, h * T:h * T + T], ALU.mult)
                    for th in range(TH):
                        ins = v.scalar_tensor_tensor(
                            ofull[:, h * T + th * 512:h * T + th * 512 + 512],
                            tscr[:, th * 512:th * 512 + 512], t_normw[:, h:h + 1],
                            banks[th][:, :], ALU.mult, ALU.mult)
                    s_dve.inc(ins)
                    wxg[h] = s_dve.v

            if debug:
                @blk.sync
                def _(s):
                    s.wait_ge(s_dve.h, wxg[HEADS - 1])
                    s_dbg.dma(s.dma_start(out=dbg_xg, in_=ofull[:]))

            ph5.close()

        if stage >= 6:
            esa = [0] * 16
            esa_base = s_act.v

            @blk.tensor
            def _(t):
                for hc in range(16):
                    banks = (pb[0], pb[1]) if hc % 2 == 0 else (pb[2], pb[3])
                    s_wo.wait(t, hc)
                    if hc == 0:
                        t.wait_ge(s_dve.h, wxg[HEADS - 1])
                    if hc >= 2:
                        t.wait_ge(s_act.h, esa_base + hc - 1)
                    for th in range(TH):
                        for ec in range(16):
                            ins = t.matmul(banks[th][:, :], wo_buf[:, (hc % 2) * 2048 + ec * 128:(hc % 2) * 2048 + ec * 128 + 128],
                                           ofull[:, ec * T + th * 512:ec * T + th * 512 + 512],
                                           start=(ec == 0), stop=(ec == 15))
                    s_wod.inc(ins)

            @blk.scalar
            def _(sc):
                for hc in range(16):
                    banks = (pb[0], pb[1]) if hc % 2 == 0 else (pb[2], pb[3])
                    sc.wait_ge(s_wod.h, hc + 1)
                    if hc >= 2:
                        s_sas.waitv(sc, hc - 2)
                    for th in range(TH):
                        ins = sc.activation(out=sa_stage[:, (hc % 2) * T + th * 512:(hc % 2) * T + th * 512 + 512],
                                            in_=banks[th][:, :], func=AF.Copy)
                    s_act.inc(ins)
                    esa[hc] = s_act.v

            @blk.sync
            def _(s):
                for hc in range(16):
                    s.wait_ge(s_act.h, esa[hc])
                    s_sas.dma(s.dma_start(out=sattn_d[hc], in_=sa_stage[:, (hc % 2) * T:(hc % 2) * T + T]), hc)

            ph56.close()
            mid.close()
            big.close()

        if stage >= 7:
            ph7 = ExitStack()
            hdn_a = ph7.enter_context(nc.sbuf_tensor("hdn_a", [128, 22 * T], BF16))
            hdn_b = ph7.enter_context(nc.sbuf_tensor("hdn_b", [128, 22 * T], BF16))
            dbuf = ph7.enter_context(nc.sbuf_tensor("dbuf", [128, 2 * IC * 128], BF16))

            def hdn_sl(ic, lo, hi):
                t_ = hdn_a if ic < 22 else hdn_b
                base = (ic if ic < 22 else ic - 22) * T
                return t_[:, base + lo:base + hi]
            gu7 = ExitStack()
            gbuf = gu7.enter_context(nc.sbuf_tensor("gbuf", [128, 2 * 2048], BF16))
            ubuf = gu7.enter_context(nc.sbuf_tensor("ubuf", [128, 2 * 2048], BF16))
            sgscr = gu7.enter_context(nc.sbuf_tensor("sgscr", [128, 2 * 512], F32))
            suscr = gu7.enter_context(nc.sbuf_tensor("suscr", [128, 2 * 512], F32))

            @blk.gpsimd
            def _(gp):
                for ic in range(2, IC):
                    if ic >= 4:
                        gp.wait_ge(s_gud.h, 2 * (ic - 1))
                    else:
                        gp.wait_ge(s_wod.h, 16)
                        s_sas.wait_all(gp)
                    s_gu.dma(gp.dma_start(out=gbuf[:, (ic % 2) * 2048:(ic % 2) * 2048 + 2048], in_=gw_d[ic]), ic)
                    s_gu.dma(gp.dma_start(out=ubuf[:, (ic % 2) * 2048:(ic % 2) * 2048 + 2048], in_=uw_d[ic]), ic)
                    if ic == 2:
                        for hc0 in (0, 1):
                            s_dn.dma(gp.dma_start(out=dbuf[:, hc0 * IC * 128:hc0 * IC * 128 + IC * 128],
                                                  in_=dwn_d[hc0]), hc0)

            ffa = [0] * (2 * IC)

            @blk.tensor
            def _(t):
                unit = 0
                for ic in range(IC):
                    s_gu.wait(t, ic)
                    for th in range(TH):
                        gbank = pb[0] if unit % 2 == 0 else pb[1]
                        ubank = pb[2] if unit % 2 == 0 else pb[3]
                        if unit >= 2:
                            t.wait_ge(s_ffa.h, 2 * (unit - 1))
                        elif ic == 0 and th == 0:
                            t.wait_ge(s_act.h, esa[15])
                        gsrc = gbuf if ic >= 2 else wkq_buf
                        usrc = ubuf if ic >= 2 else wv_buf
                        for kc in range(KC):
                            t.matmul(gbank[:, :], gsrc[:, (ic % 2) * 2048 + kc * 128:(ic % 2) * 2048 + kc * 128 + 128],
                                     xn[:, kc * T + th * 512:kc * T + th * 512 + 512],
                                     start=(kc == 0), stop=(kc == KC - 1))
                        for kc in range(KC):
                            ins = t.matmul(ubank[:, :], usrc[:, (ic % 2) * 2048 + kc * 128:(ic % 2) * 2048 + kc * 128 + 128],
                                           xn[:, kc * T + th * 512:kc * T + th * 512 + 512],
                                           start=(kc == 0), stop=(kc == KC - 1))
                        s_gud.inc(ins)
                        unit += 1

            @blk.scalar
            def _(sc):
                unit = 0
                for ic in range(IC):
                    for th in range(TH):
                        gbank = pb[0] if unit % 2 == 0 else pb[1]
                        ubank = pb[2] if unit % 2 == 0 else pb[3]
                        sc.wait_ge(s_gud.h, unit + 1)
                        if unit >= 2:
                            sc.wait_ge(s_ffd.h, unit - 1)
                        sc.activation(out=sgscr[:, (unit % 2) * 512:(unit % 2) * 512 + 512],
                                      in_=gbank[:, :], func=AF.Silu, scale=t_gs[:, ic:ic + 1])
                        ins = sc.activation(out=suscr[:, (unit % 2) * 512:(unit % 2) * 512 + 512],
                                            in_=ubank[:, :], func=AF.Copy, scale=t_us[:, ic:ic + 1])
                        s_ffa.inc(ins, 2)
                        ffa[unit] = s_ffa.v
                        unit += 1

            @blk.vector
            def _(v):
                unit = 0
                for ic in range(IC):
                    for th in range(TH):
                        v.wait_ge(s_ffa.h, ffa[unit])
                        ins = v.tensor_tensor(hdn_sl(ic, th * 512, th * 512 + 512),
                                              sgscr[:, (unit % 2) * 512:(unit % 2) * 512 + 512],
                                              suscr[:, (unit % 2) * 512:(unit % 2) * 512 + 512], ALU.mult)
                        s_ffd.inc(ins)
                        unit += 1
            w_hdn = s_ffd.v

            gu7.close()

            if stage >= 8:
                ph8 = ExitStack()
                xrbuf = ph8.enter_context(nc.sbuf_tensor("xrbuf", [128, 2 * T], F32))
                sabuf = ph8.enter_context(nc.sbuf_tensor("sabuf", [128, 2 * T], BF16))
                fscr = ph8.enter_context(nc.sbuf_tensor("fscr", [128, 2 * T], F32))
                obuf = ph8.enter_context(nc.sbuf_tensor("obuf", [128, 2 * T], F32))

                @blk.gpsimd
                def _(gp):
                    for hc in range(2, 16):
                        gp.wait_ge(s_dnd.h, hc - 1)
                        s_dn.dma(gp.dma_start(out=dbuf[:, (hc % 2) * IC * 128:(hc % 2) * IC * 128 + IC * 128],
                                              in_=dwn_d[hc]), hc)

                @blk.sync
                def _(s):
                    for hc in range(18):
                        if hc < 16:
                            if hc >= 2:
                                s.wait_ge(s_ffd2.h, hc - 1)
                            else:
                                s.wait_ge(s_ffd.h, w_hdn)
                            s_sas.wait(s, hc)
                            s_x8.dma(s.dma_start(out=xrbuf[:, (hc % 2) * T:(hc % 2) * T + T],
                                                 in_=xT[hc * 128:(hc + 1) * 128, :]), hc)
                            s_x8.dma(s.dma_start(out=sabuf[:, (hc % 2) * T:(hc % 2) * T + T], in_=sattn_d[hc]), hc)
                        if hc >= 2:
                            j = hc - 2
                            s.wait_ge(s_ffd2.h, j + 1)
                            s_out.dma(s.dma_start(out=outT[j * 128:(j + 1) * 128, :],
                                                  in_=obuf[:, (j % 2) * T:(j % 2) * T + T]), j)
                    s_out.wait_all(s)

                efd = [0] * 16
                efd_base = s_act.v

                @blk.tensor
                def _(t):
                    for hc in range(16):
                        banks = (pb[0], pb[1]) if hc % 2 == 0 else (pb[2], pb[3])
                        s_dn.wait(t, hc)
                        if hc == 0:
                            t.wait_ge(s_ffd.h, w_hdn)
                        if hc >= 2:
                            t.wait_ge(s_act.h, efd_base + hc - 1)
                        for th in range(TH):
                            for ic in range(IC):
                                ins = t.matmul(banks[th][:, :],
                                               dbuf[:, (hc % 2) * IC * 128 + ic * 128:(hc % 2) * IC * 128 + ic * 128 + 128],
                                               hdn_sl(ic, th * 512, th * 512 + 512),
                                               start=(ic == 0), stop=(ic == IC - 1))
                        s_dnd.inc(ins)

                @blk.scalar
                def _(sc):
                    for hc in range(16):
                        banks = (pb[0], pb[1]) if hc % 2 == 0 else (pb[2], pb[3])
                        sc.wait_ge(s_dnd.h, hc + 1)
                        if hc >= 2:
                            sc.wait_ge(s_ffd2.h, hc - 1)
                        for th in range(TH):
                            ins = sc.activation(out=fscr[:, (hc % 2) * T + th * 512:(hc % 2) * T + th * 512 + 512],
                                                in_=banks[th][:, :], func=AF.Silu, scale=t_ds[:, hc:hc + 1])
                        s_act.inc(ins)
                        efd[hc] = s_act.v

                @blk.vector
                def _(v):
                    for hc in range(16):
                        v.wait_ge(s_act.h, efd[hc])
                        s_x8.wait(v, hc)
                        if hc >= 2:
                            s_out.waitv(v, hc - 2)
                        sl = slice((hc % 2) * T, (hc % 2) * T + T)
                        v.tensor_tensor(fscr[:, sl], fscr[:, sl], sabuf[:, sl], ALU.mult)
                        ins = v.tensor_tensor(obuf[:, sl], fscr[:, sl], xrbuf[:, sl], ALU.add)
                        s_ffd2.inc(ins)



                ph8.close()
            ph7.close()

    if 2 <= stage < 4:
        grp.close()
        qdq_st.close()
    if stage == 5:
        ph56.close()
    if 2 <= stage < 6:
        mid.close()
    if stage < 6:
        big.close()
    es.close()
    return nc


_BUILT = {}


def kernel(**inputs):
    x = np.asarray(inputs["L0_attn_x"])
    w = host_weights(inputs)
    in_maps = []
    for core in range(8):
        b, half = core // 2, core % 2
        m = dict(w)
        m.update(host_tables(half))
        m["xT"] = _f32(x[b, half * T:(half + 1) * T, :].T)
        in_maps.append(m)
    nc = build_nc(debug=False)
    res = run_bass_kernel_spmd(nc, in_maps, core_ids=list(range(8)))
    out = np.empty((B, N, D), dtype=np.float32)
    for core in range(8):
        b, half = core // 2, core % 2
        out[b, half * T:(half + 1) * T, :] = res.results[core]["outT"].T
    return out
